# version 02
# baseline (speedup 1.0000x reference)
import numpy as np
import ml_dtypes
import concourse.bass as bass
import concourse.mybir as mybir
from concourse.bass_utils import run_bass_kernel_spmd
from contextlib import ExitStack

F32 = mybir.dt.float32
BF16 = mybir.dt.bfloat16
I32 = mybir.dt.int32
AF = mybir.ActivationFunctionType
ALU = mybir.AluOpType
AX = mybir.AxisListType

D = 2048
SF = 4096
TO = 2048
NCORES = 8
ISQ8 = 0.125
ISQ128 = float(1.0 / np.sqrt(np.float32(128.0)))
EPS = 1e-5

DEBUG_C = False
ENGS = ("sync", "gpsimd", "vector", "scalar", "tensor")
SEM_NAMES = ["SL", "GL", "TE", "VE", "SE", "PH"]


class Sched:

    def __init__(self):
        self.q = {e: [] for e in ENGS}
        self.cnt = {n: 0 for n in SEM_NAMES}

    def op(self, eng, fn, sem=None, n=1):
        if sem is None:
            self.q[eng].append(fn)
            return None
        self.cnt[sem] += n
        c = self.cnt[sem]

        def g(e, s, fn=fn, sem=sem, n=n):
            fn(e, s).then_inc(s[sem], n)

        self.q[eng].append(g)
        return c

    def dma(self, eng, out_ap, in_ap, sem=None):
        sem = sem or ("GL" if eng == "gpsimd" else "SL")
        return self.op(eng, lambda e, s, o=out_ap, i=in_ap:
                       e.dma_start(out=o, in_=i), sem=sem, n=16)

    def wait(self, eng, sem, val):
        if val:
            self.q[eng].append(
                lambda e, s, sem=sem, val=val: e.wait_ge(s[sem], val))

    def vop(self, fn):
        c = self.op("vector", fn, sem="VE")
        self.wait("vector", "VE", c)
        return c

    def sop(self, fn):
        c = self.op("scalar", fn, sem="SE")
        self.wait("scalar", "SE", c)
        return c

    def barrier(self):
        self.wait("sync", "SL", self.cnt["SL"])
        self.wait("sync", "GL", self.cnt["GL"])
        self.wait("gpsimd", "GL", self.cnt["GL"])
        self.wait("gpsimd", "SL", self.cnt["SL"])
        for e in ENGS:
            self.op(e, lambda en, s: en.nop(), sem="PH", n=1)
        t = self.cnt["PH"]
        for e in ENGS:
            self.wait(e, "PH", t)

    def finish(self):
        for name in SEM_NAMES:
            self.wait("sync", name, self.cnt[name])

    def run(self, nc, sems):
        with nc.Block() as block:
            for ename in ENGS:
                q = self.q[ename]

                def body(eng, _q=q):
                    for fn in _q:
                        fn(eng, sems)

                getattr(block, ename)(body)


def _sems(nc, ctx):
    return {n: ctx.enter_context(nc.semaphore(n)) for n in SEM_NAMES}



def emit_ln(nc, sch, sb, P0, P1, P2, P3, ones_col, ones_row,
            x_in, xn_out, stats_out=None):
    xc = [sb(f"ln_xc{i}", [128, 16 * 512], F32) for i in range(2)]
    xno = [sb(f"ln_xn{i}", [128, 16 * 512], BF16) for i in range(2)]
    sqt = [sb(f"ln_sq{i}", [128, 512], F32) for i in range(2)]
    st1 = [sb(f"ln_s1{i}", [1, 512], F32) for i in range(2)]
    st2 = [sb(f"ln_s2{i}", [1, 512], F32) for i in range(2)]
    stm = [sb(f"ln_sm{i}", [1, 512], F32) for i in range(2)]
    str_ = [sb(f"ln_sr{i}", [1, 512], F32) for i in range(2)]
    Mb = sb("ln_Mb", [128, 512], F32)
    Rb = sb("ln_Rb", [128, 512], F32)

    x_v = x_in.rearrange("(c p) t -> p c t", p=128)
    xn_v = xn_out.rearrange("(c p) t -> p c t", p=128)

    ld_done = [0, 0]
    consumed = [0, 0]
    xn_stored = [0, 0]
    st_stored = [0, 0]
    te_sq = [0, 0]
    cse_sq = [0, 0]
    prev_fin_ve = 0
    prev_bc_ve = 0

    for ch in range(8):
        b = ch % 2
        sch.wait("sync", "VE", consumed[b])
        ld_done[b] = sch.dma("sync",
                             xc[b].ap().rearrange("p (c t) -> p c t", c=16),
                             x_v[:, :, ch * 512:(ch + 1) * 512])
        sch.wait("tensor", "SL", ld_done[b])
        sch.wait("tensor", "VE", prev_fin_ve)
        for dc in range(16):
            src = xc[b][:, dc * 512:(dc + 1) * 512]
            sch.op("tensor", lambda e, s, src=src, dc=dc: e.matmul(
                P0[0:1, :], ones_col[:], src,
                start=(dc == 0), stop=(dc == 15)), sem="TE")
        sch.wait("scalar", "SL", ld_done[b])
        for dc in range(16):
            sq = sqt[dc % 2]
            src = xc[b][:, dc * 512:(dc + 1) * 512]
            sch.wait("scalar", "TE", te_sq[dc % 2])
            cse_sq[dc % 2] = sch.op(
                "scalar", lambda e, s, sq=sq, src=src: e.activation(
                    sq[:], src, AF.Square), sem="SE")
            sch.wait("tensor", "SE", cse_sq[dc % 2])
            te_sq[dc % 2] = sch.op("tensor", lambda e, s, sq=sq, dc=dc: e.matmul(
                P1[0:1, :], ones_col[:], sq[:],
                start=(dc == 0), stop=(dc == 15)), sem="TE")
        te_stats = sch.cnt["TE"]
        sch.wait("vector", "TE", te_stats)
        sch.wait("vector", "SL", st_stored[b])
        s1, s2, sm, sr = st1[b], st2[b], stm[b], str_[b]
        sch.vop(lambda e, s, s1=s1: e.tensor_scalar_mul(s1[:], P0[0:1, :], 1.0 / D))
        sch.vop(lambda e, s, s2=s2: e.tensor_scalar_mul(s2[:], P1[0:1, :], 1.0 / D))
        sch.vop(lambda e, s, sm=sm, s1=s1: e.tensor_copy(sm[:], s1[:]))
        sch.vop(lambda e, s, s1=s1: e.tensor_tensor(s1[:], s1[:], s1[:], ALU.mult))
        sch.vop(lambda e, s, s2=s2, s1=s1: e.tensor_tensor(
            s2[:], s2[:], s1[:], ALU.subtract))
        prev_fin_ve = sch.vop(lambda e, s, s2=s2: e.tensor_scalar_add(
            s2[:], s2[:], EPS))
        sch.wait("scalar", "VE", prev_fin_ve)
        sch.sop(lambda e, s, sr=sr, s2=s2: e.activation(
            sr[:], s2[:], AF.Ln))
        cs2 = sch.sop(lambda e, s, sr=sr: e.activation(
            sr[:], sr[:], AF.Exp, scale=-0.5))
        sch.wait("tensor", "SE", cs2)
        sch.wait("tensor", "VE", prev_bc_ve)
        sch.op("tensor", lambda e, s, sm=sm: e.matmul(
            P2[:], ones_row[:], sm[:], start=True, stop=True), sem="TE")
        ct2 = sch.op("tensor", lambda e, s, sr=sr: e.matmul(
            P3[:], ones_row[:], sr[:], start=True, stop=True), sem="TE")
        sch.wait("vector", "TE", ct2)
        sch.vop(lambda e, s: e.tensor_copy(Mb[:], P2[:]))
        prev_bc_ve = sch.vop(lambda e, s: e.tensor_copy(Rb[:], P3[:]))
        sch.wait("vector", "SL", xn_stored[b])
        for dc in range(16):
            xs = xc[b][:, dc * 512:(dc + 1) * 512]
            xo = xno[b][:, dc * 512:(dc + 1) * 512]
            sch.vop(lambda e, s, xs=xs: e.tensor_tensor(
                xs, xs, Mb[:], ALU.subtract))
            sch.vop(lambda e, s, xo=xo, xs=xs: e.tensor_tensor(
                xo, xs, Rb[:], ALU.mult))
        consumed[b] = sch.cnt["VE"]
        sch.wait("sync", "VE", consumed[b])
        xn_stored[b] = sch.dma("sync",
                               xn_v[:, :, ch * 512:(ch + 1) * 512],
                               xno[b].ap().rearrange("p (c t) -> p c t", c=16))
        if stats_out is not None:
            sch.wait("sync", "SE", cs2)
            sch.dma("sync", stats_out[0:1, ch * 512:(ch + 1) * 512], sm[:])
            st_stored[b] = sch.dma(
                "sync", stats_out[1:2, ch * 512:(ch + 1) * 512], sr[:])



def build_A():
    nc = bass.Bass("TRN2", target_bir_lowering=False)
    xT = nc.dram_tensor("xT", [D, SF], F32, kind="ExternalInput").ap()
    wq = nc.dram_tensor("wq", [1, 128, 16 * 1024], F32, kind="ExternalInput").ap()
    wk = nc.dram_tensor("wk", [1, 128, 16 * 256], F32, kind="ExternalInput").ap()
    wv = nc.dram_tensor("wv", [1, 128, 16 * 256], F32, kind="ExternalInput").ap()
    wgate = nc.dram_tensor("wgate", [16, 128, 16 * 512], F32, kind="ExternalInput").ap()
    wup = nc.dram_tensor("wup", [16, 128, 16 * 512], F32, kind="ExternalInput").ap()
    wdown = nc.dram_tensor("wdown", [16, 128, 64 * 128], F32, kind="ExternalInput").ap()
    masks = nc.dram_tensor("masks", [128, 4 * 512], BF16, kind="ExternalInput").ap()
    idn16 = nc.dram_tensor("idn16", [128, 128], BF16, kind="ExternalInput").ap()
    coff = nc.dram_tensor("coff", [1, 1], I32, kind="ExternalInput").ap()

    ctxp = nc.dram_tensor("ctxp", [1024, SF], BF16, kind="ExternalOutput").ap()
    ffn0 = nc.dram_tensor("ffn0", [D, TO], F32, kind="ExternalOutput").ap()

    xn_hbm = nc.dram_tensor("xn_hbm", [D, SF], BF16, kind="Internal").ap()
    hT_hbm = nc.dram_tensor("hT_hbm", [8192, TO], BF16, kind="Internal").ap()
    qT_hbm = nc.dram_tensor("qT_hbm", [1024, SF], BF16, kind="Internal").ap()

    sch = Sched()
    reg = {}

    with ExitStack() as octx:
        sems = _sems(nc, octx)
        osb = lambda *a: octx.enter_context(nc.sbuf_tensor(*a))
        ps = lambda *a: octx.enter_context(nc.psum_tensor(*a))

        s_coff = osb("s_coff", [1, 1], I32)
        s_masks = osb("s_masks", [128, 4 * 512], BF16)
        s_idn16 = osb("s_idn16", [128, 128], BF16)
        ones_col = osb("ones_col", [128, 1], F32)
        ones_row = osb("ones_row", [1, 128], F32)

        P0 = ps("P0", [128, 512], F32)
        P1 = ps("P1", [128, 512], F32)
        P2 = ps("P2", [128, 512], F32)
        P3 = ps("P3", [128, 512], F32)
        P4 = ps("P4", [128, 512], F32)
        PT16 = [ps("PT16a", [128, 128], BF16), ps("PT16b", [128, 128], BF16)]

        c0 = sch.dma("sync", s_coff[:], coff)
        c_masks = sch.dma("sync", s_masks[:], masks)
        c_idn = sch.dma("sync", s_idn16[:], idn16)
        sch.op("vector", lambda e, s: e.memset(ones_col[:], 1.0), sem="VE")
        cv0 = sch.op("vector", lambda e, s: e.memset(ones_row[:], 1.0), sem="VE")
        sch.wait("sync", "SL", c0)
        sch.op("sync", lambda e, s: reg.__setitem__(
            "coff", e.value_load(s_coff[0:1, 0:1])) or e.nop(), sem=None)
        sch.wait("tensor", "VE", cv0)

        with ExitStack() as pctx:
            sb = lambda *a: pctx.enter_context(nc.sbuf_tensor(*a))
            emit_ln(nc, sch, sb, P0, P1, P2, P3, ones_col, ones_row,
                    xT, xn_hbm)
            sch.barrier()

        with ExitStack() as pctx:
            sb = lambda *a: pctx.enter_context(nc.sbuf_tensor(*a))
            xno = sb("f_xno", [128, 16 * 2048], BF16)
            wg = sb("f_wg", [128, 16 * 512], BF16)
            wu = sb("f_wu", [128, 16 * 512], BF16)
            hfc = sb("f_hfc", [128, 8 * 512], F32)
            habs = sb("f_habs", [128, 8 * 512], F32)
            hsp = sb("f_hsp", [128, 8 * 512], BF16)
            Tge = [sb(f"f_T{i}", [128, 8 * 128], BF16) for i in range(6)]
            bacc = sb("f_b", [128, 8 * 128], F32)
            bacc2 = sb("f_b2", [128, 8 * 128], F32)
            hTst = sb("f_hTst", [128, 4 * 2048], BF16)

            xn_v = xn_hbm.rearrange("(c p) t -> p c t", p=128)
            xno_v = xno.ap().rearrange("p (c t) -> p c t", c=16)
            for cc in range(4):
                def _ldxn(e, s, cc=cc):
                    return e.dma_start(
                        out=xno_v[:, cc * 4:(cc + 1) * 4, :],
                        in_=xn_v[:, cc * 4:(cc + 1) * 4, bass.ds(reg["coff"], TO)])
                sch.op("sync", _ldxn, sem="SL", n=16)
            ld_xno = sch.cnt["SL"]

            hT_v = hT_hbm.rearrange("(c p) t -> p c t", p=128)

            te_after_fc = 0
            hT_stored = 0
            cse_prev = 0
            cvm_prev = 0
            cvlast = 0
            ve_pt = [0, 0]
            sch.wait("tensor", "SL", ld_xno)
            sch.wait("tensor", "SL", c_idn)
            for fc in range(16):
                sch.wait("gpsimd", "TE", te_after_fc)
                sch.dma("gpsimd", wg.ap().rearrange("p (c f) -> p c f", c=16),
                        wgate[fc].rearrange("p (c f) -> p c f", c=16))
                wu_ld = sch.dma("gpsimd", wu.ap().rearrange("p (c f) -> p c f", c=16),
                                wup[fc].rearrange("p (c f) -> p c f", c=16))
                sch.wait("tensor", "GL", wu_ld)
                sch.wait("vector", "SL", hT_stored)
                for half in range(2):
                    if cvlast:
                        sch.wait("scalar", "VE", cvlast)
                    for tt8 in range(8):
                        tt = half * 8 + tt8
                        sch.wait("tensor", "SE", cse_prev)
                        for dc in range(16):
                            lhsT = xno[:, dc * 2048 + tt * 128: dc * 2048 + (tt + 1) * 128]
                            rhs = wg[:, dc * 512:(dc + 1) * 512]
                            sch.op("tensor", lambda e, s, l=lhsT, r=rhs, dc=dc: e.matmul(
                                P0[:], l, r, start=(dc == 0), stop=(dc == 15)), sem="TE")
                        te_g = sch.cnt["TE"]
                        sch.wait("tensor", "VE", cvm_prev)
                        for dc in range(16):
                            lhsT = xno[:, dc * 2048 + tt * 128: dc * 2048 + (tt + 1) * 128]
                            rhs = wu[:, dc * 512:(dc + 1) * 512]
                            sch.op("tensor", lambda e, s, l=lhsT, r=rhs, dc=dc: e.matmul(
                                P1[:], l, r, start=(dc == 0), stop=(dc == 15)), sem="TE")
                        te_u = sch.cnt["TE"]
                        hslice = hfc[:, tt8 * 512:(tt8 + 1) * 512]
                        aslice = habs[:, tt8 * 512:(tt8 + 1) * 512]
                        sch.wait("scalar", "TE", te_g)
                        cse_prev = sch.op("scalar", lambda e, s, h=hslice: e.activation(
                            h, P0[:], AF.Silu), sem="SE")
                        sch.wait("vector", "SE", cse_prev)
                        sch.wait("vector", "TE", te_u)
                        cvm_prev = sch.op("vector", lambda e, s, h=hslice: e.tensor_tensor(
                            h, h, P1[:], ALU.mult), sem="VE")
                        sch.wait("scalar", "VE", cvm_prev)
                        sch.op("scalar", lambda e, s, a=aslice, h=hslice: e.activation(
                            a, h, AF.Abs), sem="SE")
                    se_half = sch.cnt["SE"]
                    Av = habs.ap().rearrange("p (g c) -> p g c", c=4)
                    Hv = hfc.ap().rearrange("p (g c) -> p g c", c=4)
                    Sv = hsp.ap().rearrange("p (g c) -> p g c", c=4)
                    sch.wait("vector", "SE", se_half)
                    pairs = [(0, 1), (0, 2), (0, 3), (1, 2), (1, 3), (2, 3)]
                    for k, (i, j) in enumerate(pairs):
                        sch.vop(lambda e, s, k=k, i=i, j=j: e.tensor_tensor(
                            Tge[k][:], Av[:, :, i], Av[:, :, j], ALU.is_ge))

                    def beat(terms, thresh, opcmp, plane):
                        (t0, sg0), rest = terms[0], terms[1:]
                        sch.vop(lambda e, s, t0=t0: e.tensor_copy(
                            bacc[:], Tge[t0][:]))
                        for (tk, sgk) in rest:
                            alu = ALU.add if sgk > 0 else ALU.subtract
                            sch.vop(lambda e, s, tk=tk, alu=alu: e.tensor_tensor(
                                bacc[:], bacc[:], Tge[tk][:], alu))
                        sch.vop(lambda e, s, th=thresh, oc=opcmp: e.tensor_scalar(
                            bacc2[:], bacc[:], th, None, oc))
                        return sch.vop(lambda e, s, pl=plane: e.tensor_tensor(
                            Sv[:, :, pl], Hv[:, :, pl], bacc2[:], ALU.mult))

                    beat([(0, 1), (1, 1), (2, 1)], 1.5, ALU.is_ge, 0)
                    beat([(3, 1), (4, 1), (0, -1)], 0.5, ALU.is_ge, 1)
                    beat([(5, 1), (1, -1), (3, -1)], -0.5, ALU.is_ge, 2)
                    cvlast = beat([(2, 1), (4, 1), (5, 1)], 1.5, ALU.is_le, 3)
                    sch.wait("tensor", "VE", cvlast)
                    for tt8 in range(8):
                        tt = half * 8 + tt8
                        for il in range(4):
                            pt = (tt8 * 4 + il) % 2
                            src = hsp[:, tt8 * 512 + il * 128: tt8 * 512 + (il + 1) * 128]
                            dst = hTst[:, il * 2048 + tt * 128: il * 2048 + (tt + 1) * 128]
                            sch.wait("tensor", "VE", ve_pt[pt])
                            ctr = sch.op("tensor", lambda e, s, src=src, pt=pt: e.transpose(
                                PT16[pt][:], src, s_idn16[:]), sem="TE")
                            sch.wait("vector", "TE", ctr)
                            ve_pt[pt] = sch.op("vector", lambda e, s, dst=dst, pt=pt:
                                               e.tensor_copy(dst, PT16[pt][:]), sem="VE")
                te_after_fc = sch.cnt["TE"]
                cv_fc = sch.cnt["VE"]
                sch.wait("sync", "VE", cv_fc)
                hT_stored = sch.dma("sync", hT_v[:, fc * 4:(fc + 1) * 4, :],
                                    hTst.ap().rearrange("p (c t) -> p c t", c=4))
            sch.barrier()

        with ExitStack() as pctx:
            sb = lambda *a: pctx.enter_context(nc.sbuf_tensor(*a))
            hTc = sb("d_hT", [128, 64 * 1024], BF16)
            wd = [sb(f"d_wd{i}", [128, 64 * 128], BF16) for i in range(2)]
            ost = [sb(f"d_ost{i}", [128, 1024], F32) for i in range(2)]

            hT_v = hT_hbm.rearrange("(c p) t -> p c t", p=128)
            ffn_v = ffn0.rearrange("(c p) t -> p c t", p=128)
            hTc_v = hTc.ap().rearrange("p (c t) -> p c t", c=64)

            te_wd = [0, 0]
            ost_st = [0, 0]
            ve_evac = [0, 0]
            for tch in range(2):
                sch.wait("sync", "TE", sch.cnt["TE"])
                ld_h = 0
                for part in range(8):
                    ld_h = sch.dma("sync", hTc_v[:, part * 8:(part + 1) * 8, :],
                                   hT_v[:, part * 8:(part + 1) * 8,
                                        tch * 1024:(tch + 1) * 1024])
                sch.wait("tensor", "SL", ld_h)
                for fo in range(16):
                    wb = fo % 2
                    sch.wait("gpsimd", "TE", te_wd[wb])
                    wld = sch.dma("gpsimd", wd[wb][:], wdown[fo])
                    sch.wait("tensor", "GL", wld)
                    for sub in range(2):
                        pbank = P0 if sub == 0 else P1
                        sch.wait("tensor", "VE", ve_evac[sub])
                        for ic in range(64):
                            lhsT = wd[wb][:, ic * 128:(ic + 1) * 128]
                            rhs = hTc[:, ic * 1024 + sub * 512: ic * 1024 + (sub + 1) * 512]
                            sch.op("tensor", lambda e, s, l=lhsT, r=rhs, ic=ic, pb=pbank:
                                   e.matmul(pb[:], l, r, start=(ic == 0), stop=(ic == 63)),
                                   sem="TE")
                        te_wd[wb] = sch.cnt["TE"]
                        ob = ost[wb]
                        sch.wait("vector", "TE", te_wd[wb])
                        sch.wait("vector", "SL", ost_st[wb])
                        ve_evac[sub] = sch.op(
                            "vector", lambda e, s, ob=ob, sub=sub, pb=pbank: e.tensor_copy(
                                ob[:, sub * 512:(sub + 1) * 512], pb[:]), sem="VE")
                    sch.wait("sync", "VE", ve_evac[1])
                    ost_st[wb] = sch.dma(
                        "sync", ffn_v[:, fo, tch * 1024:(tch + 1) * 1024], ost[wb][:])
            sch.barrier()

        with ExitStack() as pctx:
            sb = lambda *a: pctx.enter_context(nc.sbuf_tensor(*a))
            wq_sb = sb("q_wq", [128, 16 * 1024], BF16)
            wk_sb = sb("q_wk", [128, 16 * 256], BF16)
            wv_sb = sb("q_wv", [128, 16 * 256], BF16)
            xnc = [sb(f"q_xn{i}", [128, 16 * 512], BF16) for i in range(2)]
            KTp = [sb(f"q_KT{i}", [128, SF], BF16) for i in range(2)]
            V_sb = sb("q_V", [128, 32 * 4 * 65], BF16)
            qst = [sb(f"q_qs{i}", [128, 8 * 512], BF16) for i in range(2)]

            sch.dma("gpsimd", wq_sb[:], wq[0])
            sch.dma("gpsimd", wk_sb[:], wk[0])
            wld = sch.dma("gpsimd", wv_sb[:], wv[0])
            Vv = V_sb.ap().rearrange("p (n c) -> p n c", c=65)
            cvo = sch.op("vector", lambda e, s: e.memset(Vv[:, :, 64], 1.0), sem="VE")

            xn_v = xn_hbm.rearrange("(c p) t -> p c t", p=128)
            qT_v = qT_hbm.rearrange("(c p) t -> p c t", p=128)
            sch.wait("tensor", "GL", wld)
            sch.wait("tensor", "VE", cvo)

            ld_x = [0, 0]
            q_stored = [0, 0]
            te_x = [0, 0]
            ve_p = [0, 0, 0]
            for ch in range(8):
                b = ch % 2
                sch.wait("sync", "TE", te_x[b])
                ld_x[b] = sch.dma("sync", xnc[b].ap().rearrange("p (c t) -> p c t", c=16),
                                  xn_v[:, :, ch * 512:(ch + 1) * 512])
                sch.wait("tensor", "SL", ld_x[b])
                sch.wait("vector", "SL", q_stored[b])
                for ft in range(8):
                    pb = [P0, P1][ft % 2]
                    sch.wait("tensor", "VE", ve_p[ft % 2])
                    for dc in range(16):
                        lhsT = wq_sb[:, dc * 1024 + ft * 128: dc * 1024 + (ft + 1) * 128]
                        rhs = xnc[b][:, dc * 512:(dc + 1) * 512]
                        sch.op("tensor", lambda e, s, l=lhsT, r=rhs, dc=dc, pb=pb: e.matmul(
                            pb[:], l, r, start=(dc == 0), stop=(dc == 15)), sem="TE")
                    cte = sch.cnt["TE"]
                    dst = qst[b][:, ft * 512:(ft + 1) * 512]
                    sch.wait("vector", "TE", cte)
                    ve_p[ft % 2] = sch.op("vector", lambda e, s, dst=dst, pb=pb:
                                          e.tensor_copy(dst, pb[:]), sem="VE")
                sch.wait("sync", "VE", ve_p[1])
                q_stored[b] = sch.dma("sync", qT_v[:, :, ch * 512:(ch + 1) * 512],
                                      qst[b].ap().rearrange("p (c t) -> p c t", c=8))
                for ft in range(2):
                    pb = [P0, P1][ft % 2]
                    sch.wait("tensor", "VE", ve_p[ft % 2])
                    for dc in range(16):
                        lhsT = wk_sb[:, dc * 256 + ft * 128: dc * 256 + (ft + 1) * 128]
                        rhs = xnc[b][:, dc * 512:(dc + 1) * 512]
                        sch.op("tensor", lambda e, s, l=lhsT, r=rhs, dc=dc, pb=pb: e.matmul(
                            pb[:], l, r, start=(dc == 0), stop=(dc == 15)), sem="TE")
                    cte = sch.cnt["TE"]
                    dst = KTp[ft][:, ch * 512:(ch + 1) * 512]
                    sch.wait("vector", "TE", cte)
                    ve_p[ft % 2] = sch.op("vector", lambda e, s, dst=dst, pb=pb:
                                          e.tensor_copy(dst, pb[:]), sem="VE")
                for tt in range(4):
                    kt = ch * 4 + tt
                    sch.wait("tensor", "VE", ve_p[2])
                    for dc in range(16):
                        lhsT = xnc[b][:, dc * 512 + tt * 128: dc * 512 + (tt + 1) * 128]
                        rhs = wv_sb[:, dc * 256:(dc + 1) * 256]
                        sch.op("tensor", lambda e, s, l=lhsT, r=rhs, dc=dc: e.matmul(
                            P2[:, 0:256], l, r, start=(dc == 0), stop=(dc == 15)), sem="TE")
                    cte = sch.cnt["TE"]
                    sch.wait("vector", "TE", cte)
                    for gl in range(4):
                        dst = V_sb[:, (kt * 4 + gl) * 65: (kt * 4 + gl) * 65 + 64]
                        ve_p[2] = sch.op("vector", lambda e, s, dst=dst, gl=gl:
                                         e.tensor_copy(dst, P2[:, gl * 64:(gl + 1) * 64]),
                                         sem="VE")
                te_x[b] = sch.cnt["TE"]

            sch.barrier()

            Qd = [sb(f"a_Qd{i}", [128, 512], BF16) for i in range(2)]
            Eb = [sb(f"a_Eb{i}", [128, 512], BF16) for i in range(2)]
            invZ = sb("a_invZ", [1, 512], F32)
            invZb = sb("a_invZb", [64, 512], F32)
            ctxst = [sb(f"a_cst{i}", [64, SF], BF16) for i in range(2)]

            PCTX = [P2, P3]
            mm_s = [0, 0]
            exp_c = [0, 0]
            ctx_mm = [0, 0]
            qd_used = [0, 0]
            ctx_free = [0, 0]
            cst_stored = [0, 0]
            ve_bc = 0
            sch.wait("vector", "SL", c_masks)
            for hd in range(16):
                gl, j = hd // 4, hd % 4
                hr = gl * 256 + j * 64
                cb = hd % 2
                cs = hd % 2
                sch.wait("vector", "SL", cst_stored[cs])
                for qc in range(8):
                    qb = qc % 2
                    sch.wait("sync", "TE", qd_used[qb])
                    sch.dma("sync", Qd[qb][0:64, :],
                            qT_hbm[hr:hr + 64, qc * 512:(qc + 1) * 512])
                    ldq = sch.dma("sync", Qd[qb][64:128, :],
                                  qT_hbm[hr:hr + 64, qc * 512:(qc + 1) * 512])
                    sch.wait("tensor", "SL", ldq)
                    nkt = 4 * qc + 4
                    slot = gl % 2
                    for kt in range(nkt):
                        eb = kt % 2
                        pbank = [P0, P1][eb]
                        lhsT = KTp[gl // 2][slot * 64: slot * 64 + 64,
                                            kt * 128:(kt + 1) * 128]
                        rhs = Qd[qb][slot * 64: slot * 64 + 64, :]
                        sch.wait("tensor", "SE", exp_c[eb])
                        sch.op("tensor", lambda e, s, l=lhsT, r=rhs, pb=pbank: e.matmul(
                            pb[:], l, r, start=True, stop=True), sem="TE")
                        mm_s[eb] = sch.cnt["TE"]
                        masked = kt >= nkt - 4
                        sch.wait("scalar", "TE", mm_s[eb])
                        sch.wait("scalar", "TE", ctx_mm[eb])
                        ebuf = Eb[eb]
                        ce = sch.op("scalar", lambda e, s, eb2=ebuf, pb=pbank: e.activation(
                            eb2[:], pb[:], AF.Exp, scale=ISQ8), sem="SE")
                        if masked:
                            p = kt - (nkt - 4)
                            msl = s_masks[:, p * 512:(p + 1) * 512]
                            sch.wait("vector", "SE", ce)
                            cm = sch.op("vector", lambda e, s, eb2=ebuf, m=msl:
                                        e.tensor_tensor(eb2[:], eb2[:], m, ALU.mult),
                                        sem="VE")
                            sch.wait("tensor", "VE", cm)
                        else:
                            sch.wait("tensor", "SE", ce)
                        exp_c[eb] = ce
                        vblk = V_sb[:, (kt * 4 + gl) * 65: (kt * 4 + gl + 1) * 65]
                        if kt == 0:
                            sch.wait("tensor", "VE", ctx_free[cb])
                        sch.op("tensor", lambda e, s, v=vblk, eb2=ebuf, pc=PCTX[cb],
                               st=(kt == 0), sp=(kt == nkt - 1): e.matmul(
                            pc[0:65, :], v, eb2[:], start=st, stop=sp), sem="TE")
                        ctx_mm[eb] = sch.cnt["TE"]
                    qd_used[qb] = sch.cnt["TE"]
                    te_last = sch.cnt["TE"]
                    sch.wait("vector", "TE", te_last)
                    cvr = sch.op("vector", lambda e, s, pc=PCTX[cb]: e.reciprocal(
                        invZ[:], pc[64:65, :]), sem="VE")
                    sch.wait("tensor", "VE", cvr)
                    sch.wait("tensor", "VE", ve_bc)
                    ctb = sch.op("tensor", lambda e, s: e.matmul(
                        P4[0:64, :], ones_row[0:1, 0:64], invZ[:],
                        start=True, stop=True), sem="TE")
                    sch.wait("vector", "TE", ctb)
                    ve_bc = sch.vop(lambda e, s: e.tensor_copy(
                        invZb[:], P4[0:64, :]))
                    dst = ctxst[cs][0:64, qc * 512:(qc + 1) * 512]
                    ctx_free[cb] = sch.op(
                        "vector", lambda e, s, dst=dst, pc=PCTX[cb]: e.tensor_tensor(
                            dst, pc[0:64, :], invZb[:], ALU.mult), sem="VE")
                sch.wait("sync", "VE", ctx_free[cb])
                cst_stored[cs] = sch.dma("sync", ctxp[hr:hr + 64, :], ctxst[cs][0:64, :])

        sch.finish()
        sch.run(nc, sems)
    return nc



def build_B():
    nc = bass.Bass("TRN2", target_bir_lowering=False)
    ctxb = nc.dram_tensor("ctxb", [1, 128, 16 * TO], BF16, kind="ExternalInput").ap()
    wo = nc.dram_tensor("wo", [16, 128, 16 * 128], F32, kind="ExternalInput").ap()
    xo = nc.dram_tensor("xo", [D, TO], F32, kind="ExternalInput").ap()
    ffn = nc.dram_tensor("ffn", [D, TO], F32, kind="ExternalInput").ap()
    x1 = nc.dram_tensor("x1", [D, TO], F32, kind="ExternalOutput").ap()

    sch = Sched()
    with ExitStack() as octx:
        sems = _sems(nc, octx)
        sb = lambda *a: octx.enter_context(nc.sbuf_tensor(*a))
        ps = lambda *a: octx.enter_context(nc.psum_tensor(*a))
        ctx_sb = sb("b_ctx", [128, 16 * TO], BF16)
        wo_sb = [sb(f"b_wo{i}", [128, 16 * 128], BF16) for i in range(2)]
        xfo = [sb(f"b_x{i}", [128, TO], F32) for i in range(2)]
        ffo = [sb(f"b_f{i}", [128, TO], F32) for i in range(2)]
        x1st = [sb(f"b_o{i}", [128, TO], F32) for i in range(2)]
        P0 = ps("P0", [128, 512], F32)
        P1 = ps("P1", [128, 512], F32)

        ld_ctx = sch.dma("sync", ctx_sb[:], ctxb[0])
        sch.wait("tensor", "SL", ld_ctx)

        xo_v = xo.rearrange("(c p) t -> p c t", p=128)
        ffn_v = ffn.rearrange("(c p) t -> p c t", p=128)
        x1_v = x1.rearrange("(c p) t -> p c t", p=128)

        te_w = [0, 0]
        ld_xf = [0, 0]
        out_st = [0, 0]
        ve_p = [0, 0]
        for fo in range(16):
            wb = fo % 2
            sch.wait("gpsimd", "TE", te_w[wb])
            wld = sch.dma("gpsimd", wo_sb[wb][:], wo[fo])
            sch.wait("sync", "VE", ve_p[1])
            sch.dma("sync", xfo[wb][:], xo_v[:, fo, :])
            ld_xf[wb] = sch.dma("sync", ffo[wb][:], ffn_v[:, fo, :])
            sch.wait("tensor", "GL", wld)
            for qc in range(4):
                pb = [P0, P1][qc % 2]
                sch.wait("tensor", "VE", ve_p[qc % 2])
                for cc in range(16):
                    lhsT = wo_sb[wb][:, cc * 128:(cc + 1) * 128]
                    rhs = ctx_sb[:, cc * TO + qc * 512: cc * TO + (qc + 1) * 512]
                    sch.op("tensor", lambda e, s, l=lhsT, r=rhs, cc=cc, pb=pb: e.matmul(
                        pb[:], l, r, start=(cc == 0), stop=(cc == 15)), sem="TE")
                te_w[wb] = sch.cnt["TE"]
                sch.wait("vector", "TE", te_w[wb])
                sch.wait("vector", "SL", ld_xf[wb])
                sch.wait("vector", "SL", out_st[wb])
                dst = x1st[wb][:, qc * 512:(qc + 1) * 512]
                xs = xfo[wb][:, qc * 512:(qc + 1) * 512]
                fs = ffo[wb][:, qc * 512:(qc + 1) * 512]
                sch.vop(lambda e, s, dst=dst, xs=xs, pb=pb: e.tensor_tensor(
                    dst, pb[:], xs, ALU.add))
                ve_p[qc % 2] = sch.vop(
                    lambda e, s, dst=dst, fs=fs: e.tensor_tensor(
                        dst, dst, fs, ALU.add))
            sch.wait("sync", "VE", ve_p[1])
            out_st[wb] = sch.dma("sync", x1_v[:, fo, :], x1st[wb][:])
        sch.finish()
        sch.run(nc, sems)
    return nc



def build_C():
    nc = bass.Bass("TRN2", target_bir_lowering=False)
    x1T = nc.dram_tensor("x1T", [D, SF], F32, kind="ExternalInput").ap()
    x1o = nc.dram_tensor("x1o", [16, 128, 16 * 128], F32, kind="ExternalInput").ap()
    wq1 = nc.dram_tensor("wq1", [1, 128, 16 * 1024], F32, kind="ExternalInput").ap()
    wk1 = nc.dram_tensor("wk1", [1, 128, 16 * 1024], F32, kind="ExternalInput").ap()
    wv1 = nc.dram_tensor("wv1", [1, 128, 16 * 1024], F32, kind="ExternalInput").ap()
    gw = nc.dram_tensor("gw", [1, 128, 16 * 8], F32, kind="ExternalInput").ap()
    wgm = nc.dram_tensor("wgm", [8, 11, 128, 16 * 128], F32, kind="ExternalInput").ap()
    wum = nc.dram_tensor("wum", [8, 11, 128, 16 * 128], F32, kind="ExternalInput").ap()
    wdm = nc.dram_tensor("wdm", [8, 16, 128, 11 * 128], F32, kind="ExternalInput").ap()
    swg = nc.dram_tensor("swg", [22, 128, 16 * 128], F32, kind="ExternalInput").ap()
    swu = nc.dram_tensor("swu", [22, 128, 16 * 128], F32, kind="ExternalInput").ap()
    swd = nc.dram_tensor("swd", [16, 128, 22 * 128], F32, kind="ExternalInput").ap()
    idn32 = nc.dram_tensor("idn32", [128, 128], F32, kind="ExternalInput").ap()
    coff = nc.dram_tensor("coff", [1, 1], I32, kind="ExternalInput").ap()

    valp = nc.dram_tensor("valp", [1024, SF], BF16, kind="ExternalOutput").ap()
    moeT = nc.dram_tensor("moeT", [D, TO], F32, kind="ExternalOutput").ap()

    x1n_hbm = nc.dram_tensor("x1n_hbm", [D, SF], BF16, kind="Internal").ap()
    qT_hbm = nc.dram_tensor("qT1_hbm", [1024, SF], BF16, kind="Internal").ap()
    kT_hbm = nc.dram_tensor("kT1_hbm", [1024, SF], BF16, kind="Internal").ap()
    vT_hbm = nc.dram_tensor("vT1_hbm", [1024, SF], BF16, kind="Internal").ap()
    stats_hbm = nc.dram_tensor("stats_hbm", [2, SF], F32, kind="Internal").ap()
    scol_hbm = nc.dram_tensor("scol_hbm", [8, SF], F32, kind="Internal").ap()
    combT_hbm = nc.dram_tensor("combT_hbm", [1, 8 * TO], F32, kind="Internal").ap()
    if DEBUG_C:
        dbg = {n: nc.dram_tensor("dbg_" + n, sh, dt, kind="ExternalOutput").ap()
               for n, sh, dt in [("xn", [D, SF], BF16), ("q", [1024, SF], BF16),
                                 ("k", [1024, SF], BF16), ("v", [1024, SF], BF16),
                                 ("scol", [8, SF], F32), ("comb", [1, 8 * TO], F32),
                                 ("stats", [2, SF], F32), ("combsb", [128, 128], F32),
                                 ("gate0", [128, 8 * 8], F32), ("zz", [128, 256], F32),
                                 ("praw", [128, 16 * 8], F32), ("pmr", [128, 16 * 2], F32)]}

    sch = Sched()
    reg = {}

    with ExitStack() as octx:
        sems = _sems(nc, octx)
        osb = lambda *a: octx.enter_context(nc.sbuf_tensor(*a))
        ps = lambda *a: octx.enter_context(nc.psum_tensor(*a))

        s_coff = osb("s_coff", [1, 1], I32)
        s_idn32 = osb("s_idn32", [128, 128], F32)
        ones_col = osb("ones_col", [128, 1], F32)
        ones_row = osb("ones_row", [1, 128], F32)
        comb_sb = osb("comb", [128, 16 * 8], F32)

        P0 = ps("P0", [128, 512], F32)
        P1 = ps("P1", [128, 512], F32)
        P2 = ps("P2", [128, 512], F32)
        P3 = ps("P3", [128, 512], F32)

        c0 = sch.dma("sync", s_coff[:], coff)
        c_idn = sch.dma("sync", s_idn32[:], idn32)
        sch.op("vector", lambda e, s: e.memset(ones_col[:], 1.0), sem="VE")
        cv0 = sch.op("vector", lambda e, s: e.memset(ones_row[:], 1.0), sem="VE")
        sch.wait("sync", "SL", c0)
        sch.op("sync", lambda e, s: reg.__setitem__(
            "coff", e.value_load(s_coff[0:1, 0:1])) or e.nop(), sem=None)
        sch.wait("tensor", "VE", cv0)

        with ExitStack() as pctx:
            sb = lambda *a: pctx.enter_context(nc.sbuf_tensor(*a))
            emit_ln(nc, sch, sb, P0, P1, P2, P3, ones_col, ones_row,
                    x1T, x1n_hbm, stats_out=stats_hbm)
            sch.barrier()

        with ExitStack() as pctx:
            sb = lambda *a: pctx.enter_context(nc.sbuf_tensor(*a))
            gw_sb = sb("g_gw", [128, 16 * 8], F32)
            sow = sb("g_sow", [2, TO], F32)
            Sgb = sb("g_Sgb", [128, 8], F32)
            xt = [sb(f"g_xt{i}", [128, 16 * 128], F32) for i in range(2)]
            mr = sb("g_mr", [128, 2], F32)
            t8a = sb("g_t8a", [128, 8], F32)
            t8b = sb("g_t8b", [128, 8], F32)
            e8 = sb("g_e8", [128, 8], F32)
            z8 = sb("g_z8", [128, 1], F32)
            mx8 = sb("g_mx8", [128, 1], F32)
            sc8 = sb("g_sc8", [128, 8], F32)
            cmpA = sb("g_cmpA", [128, 8], F32)
            cmpB = sb("g_cmpB", [128, 8], F32)
            bts = sb("g_bts", [128, 8], F32)
            r1 = sb("g_r1", [128, 1], F32)
            r2 = sb("g_r2", [128, 1], F32)
            if DEBUG_C:
                praw_sb = sb("g_praw", [128, 16 * 8], F32)
                pmr_sb = sb("g_pmr", [128, 16 * 2], F32)

            ldg = sch.dma("sync", gw_sb[:], gw[0])

            def _ldsow(e, s):
                return e.dma_start(out=sow[:],
                                   in_=stats_hbm[:, bass.ds(reg["coff"], TO)])
            ldsow = sch.op("sync", _ldsow, sem="SL", n=16)

            sch.wait("tensor", "SL", ldg)
            for dc in range(16):
                rhs = gw_sb[:, dc * 8:(dc + 1) * 8]
                sch.op("tensor", lambda e, s, r=rhs, dc=dc: e.matmul(
                    P0[0:1, 0:8], ones_col[:], r,
                    start=(dc == 0), stop=(dc == 15)), sem="TE")
            cte = sch.cnt["TE"]
            sch.wait("vector", "TE", cte)
            cvs = sch.op("vector", lambda e, s: e.tensor_copy(t8a[0:1, :], P0[0:1, 0:8]), sem="VE")
            sch.wait("tensor", "VE", cvs)
            ctb = sch.op("tensor", lambda e, s: e.matmul(
                P1[:, 0:8], ones_row[:], t8a[0:1, :], start=True, stop=True), sem="TE")
            sch.wait("vector", "TE", ctb)
            cvsg = sch.vop(lambda e, s: e.tensor_copy(Sgb[:], P1[:, 0:8]))

            sch.wait("tensor", "SL", ldsow)
            sch.wait("tensor", "SL", c_idn)
            ld_xt = [0, 0]
            te_xt = [0, 0]
            ve_l = 0
            ve_mr = 0
            for tt in range(16):
                b = tt % 2
                sch.wait("sync", "TE", te_xt[b])
                ld_xt[b] = sch.dma("sync", xt[b][:], x1o[tt])
                sch.wait("tensor", "SL", ld_xt[b])
                sch.wait("tensor", "VE", ve_l)
                for dc in range(16):
                    lhsT = xt[b][:, dc * 128:(dc + 1) * 128]
                    rhs = gw_sb[:, dc * 8:(dc + 1) * 8]
                    sch.op("tensor", lambda e, s, l=lhsT, r=rhs, dc=dc: e.matmul(
                        P2[:, 0:8], l, r, start=(dc == 0), stop=(dc == 15)), sem="TE")
                te_xt[b] = sch.cnt["TE"]
                src = sow[0:2, tt * 128:(tt + 1) * 128]
                sch.wait("tensor", "VE", ve_mr)
                ctr = sch.op("tensor", lambda e, s, src=src: e.transpose(
                    P3[0:128, 0:2], src, s_idn32[0:2, 0:2]), sem="TE")
                sch.wait("vector", "TE", ctr)
                ve_mr = sch.vop(lambda e, s: e.tensor_copy(mr[:], P3[0:128, 0:2]))
                if DEBUG_C:
                    sch.vop(lambda e, s, tt=tt: e.tensor_copy(
                        praw_sb[:, tt * 8:(tt + 1) * 8], P2[:, 0:8]))
                    sch.vop(lambda e, s, tt=tt: e.tensor_copy(
                        pmr_sb[:, tt * 2:(tt + 1) * 2], mr[:]))
                sch.vop(lambda e, s: e.tensor_scalar(
                    t8a[:], Sgb[:], mr[:, 0:1], None, ALU.mult))
                sch.vop(lambda e, s: e.tensor_tensor(
                    t8b[:], P2[:, 0:8], t8a[:], ALU.subtract))
                ve_l = sch.vop(lambda e, s: e.tensor_scalar(
                    t8b[:], t8b[:], mr[:, 1:2], None, ALU.mult))
                sch.vop(lambda e, s: e.reduce_max(mx8[:], t8b[:], axis=AX.X))
                cvm = sch.vop(lambda e, s: e.tensor_scalar_mul(mx8[:], mx8[:], -1.0))
                sch.wait("scalar", "VE", cvm)
                cse = sch.op("scalar", lambda e, s: e.activation(
                    e8[:], t8b[:], AF.Exp, bias=mx8[:, 0:1], accum_out=z8[:]), sem="SE")
                sch.wait("vector", "SE", cse)
                sch.vop(lambda e, s: e.reciprocal(z8[:], z8[:]))
                sch.vop(lambda e, s: e.tensor_scalar(
                    sc8[:], e8[:], z8[:, 0:1], None, ALU.mult))
                for ei in range(8):
                    sch.vop(lambda e, s, ei=ei: e.tensor_scalar(
                        cmpA[:], sc8[:], sc8[:, ei:ei + 1], None, ALU.is_lt))
                    sch.vop(lambda e, s, ei=ei: e.tensor_scalar(
                        cmpB[:], sc8[:], sc8[:, ei:ei + 1], None, ALU.is_le))
                    if ei == 0:
                        sch.vop(lambda e, s, ei=ei: e.reduce_sum(
                            bts[:, ei:ei + 1], cmpB[:, 1:8], axis=AX.X))
                    elif ei == 7:
                        sch.vop(lambda e, s, ei=ei: e.reduce_sum(
                            bts[:, ei:ei + 1], cmpA[:, 0:7], axis=AX.X))
                    else:
                        sch.vop(lambda e, s, ei=ei: e.reduce_sum(
                            r1[:], cmpA[:, 0:ei], axis=AX.X))
                        sch.vop(lambda e, s, ei=ei: e.reduce_sum(
                            r2[:], cmpB[:, ei + 1:8], axis=AX.X))
                        sch.vop(lambda e, s, ei=ei: e.tensor_tensor(
                            bts[:, ei:ei + 1], r1[:], r2[:], ALU.add))
                if DEBUG_C and tt == 0:
                    g0 = pctx.enter_context(nc.sbuf_tensor("g_dbg0", [128, 8 * 8], F32))
                    sch.op("vector", lambda e, s: e.tensor_copy(g0[:, 0:8], P2[:, 0:8]), sem="VE")
                    sch.op("vector", lambda e, s: e.tensor_copy(g0[:, 8:10], mr[:]), sem="VE")
                    sch.op("vector", lambda e, s: e.tensor_copy(g0[:, 16:24], t8a[:]), sem="VE")
                    sch.op("vector", lambda e, s: e.tensor_copy(g0[:, 24:32], t8b[:]), sem="VE")
                    sch.op("vector", lambda e, s: e.tensor_copy(g0[:, 32:40], e8[:]), sem="VE")
                    sch.op("vector", lambda e, s: e.tensor_copy(g0[:, 40:41], z8[:]), sem="VE")
                    sch.op("vector", lambda e, s: e.tensor_copy(g0[:, 48:56], sc8[:]), sem="VE")
                    cdb = sch.op("vector", lambda e, s: e.tensor_copy(g0[:, 56:64], bts[:]), sem="VE")
                    sch.wait("sync", "VE", cdb)
                    sch.dma("sync", dbg["gate0"], g0[:])
                sch.vop(lambda e, s: e.tensor_scalar(
                    bts[:], bts[:], 5.5, None, ALU.is_ge))
                dstc = comb_sb[:, tt * 8:(tt + 1) * 8]
                ve_l = sch.vop(lambda e, s, dstc=dstc: e.tensor_tensor(
                    dstc, sc8[:], bts[:], ALU.mult))
            if DEBUG_C:
                sch.wait("sync", "VE", ve_l)
                sch.dma("sync", dbg["combsb"], comb_sb[:])
                sch.dma("sync", dbg["praw"], praw_sb[:])
                sch.dma("sync", dbg["pmr"], pmr_sb[:])
            combT8 = sb("g_cT8", [8, 128], F32)
            combT_v = combT_hbm.rearrange("a (h t) -> (a h) t", h=8)
            ve_ct = 0
            ct_st = 0
            for tt in range(16):
                src = comb_sb[:, tt * 8:(tt + 1) * 8]
                sch.wait("tensor", "VE", max(ve_l, ve_ct))
                ctr = sch.op("tensor", lambda e, s, src=src: e.transpose(
                    P3[0:8, 0:128], src, s_idn32[:]), sem="TE")
                sch.wait("vector", "TE", ctr)
                sch.wait("vector", "SL", ct_st)
                ve_ct = sch.op("vector", lambda e, s: e.tensor_copy(
                    combT8[:], P3[0:8, 0:128]), sem="VE")
                sch.wait("sync", "VE", ve_ct)
                ct_st = sch.dma("sync", combT_v[:, tt * 128:(tt + 1) * 128],
                                combT8[:])
            sch.barrier()

        with ExitStack() as pctx:
            sb = lambda *a: pctx.enter_context(nc.sbuf_tensor(*a))
            wq_sb = sb("c3_wq", [128, 16 * 1024], BF16)
            wk_sb = sb("c3_wk", [128, 16 * 1024], BF16)
            wv_sb = sb("c3_wv", [128, 16 * 1024], BF16)
            xnc = [sb(f"c3_xn{i}", [128, 16 * 512], BF16) for i in range(2)]
            qstg = [sb(f"c3_qs{i}", [128, 8 * 512], BF16) for i in range(2)]
            kstg = [sb(f"c3_ks{i}", [128, 8 * 512], BF16) for i in range(2)]
            vstg = [sb(f"c3_vs{i}", [128, 8 * 512], BF16) for i in range(2)]

            sch.dma("gpsimd", wq_sb[:], wq1[0])
            sch.dma("gpsimd", wk_sb[:], wk1[0])
            wld = sch.dma("gpsimd", wv_sb[:], wv1[0])
            xn_v = x1n_hbm.rearrange("(c p) t -> p c t", p=128)
            views = [(wq_sb, qT_hbm.rearrange("(c p) t -> p c t", p=128), qstg),
                     (wk_sb, kT_hbm.rearrange("(c p) t -> p c t", p=128), kstg),
                     (wv_sb, vT_hbm.rearrange("(c p) t -> p c t", p=128), vstg)]
            sch.wait("tensor", "GL", wld)
            ld_x = [0, 0]
            te_x = [0, 0]
            st_done = {}
            ve_p = [0, 0]
            for ch in range(8):
                b = ch % 2
                sch.wait("sync", "TE", te_x[b])
                ld_x[b] = sch.dma("sync", xnc[b].ap().rearrange("p (c t) -> p c t", c=16),
                                  xn_v[:, :, ch * 512:(ch + 1) * 512])
                sch.wait("tensor", "SL", ld_x[b])
                for wi, (wsb, outv, stg) in enumerate(views):
                    sch.wait("vector", "SL", st_done.get((wi, b), 0))
                    for ft in range(8):
                        pb = [P0, P1][ft % 2]
                        sch.wait("tensor", "VE", ve_p[ft % 2])
                        for dc in range(16):
                            lhsT = wsb[:, dc * 1024 + ft * 128: dc * 1024 + (ft + 1) * 128]
                            rhs = xnc[b][:, dc * 512:(dc + 1) * 512]
                            sch.op("tensor", lambda e, s, l=lhsT, r=rhs, dc=dc, pb=pb:
                                   e.matmul(pb[:], l, r, start=(dc == 0), stop=(dc == 15)),
                                   sem="TE")
                        cte = sch.cnt["TE"]
                        dst = stg[b][:, ft * 512:(ft + 1) * 512]
                        sch.wait("vector", "TE", cte)
                        ve_p[ft % 2] = sch.op("vector", lambda e, s, dst=dst, pb=pb:
                                              e.tensor_copy(dst, pb[:]), sem="VE")
                    sch.wait("sync", "VE", ve_p[1])
                    st_done[(wi, b)] = sch.dma(
                        "sync", outv[:, :, ch * 512:(ch + 1) * 512],
                        stg[b].ap().rearrange("p (c t) -> p c t", c=8))
                te_x[b] = sch.cnt["TE"]
            sch.barrier()

        with ExitStack() as pctx:
            sb = lambda *a: pctx.enter_context(nc.sbuf_tensor(*a))
            q_sb = sb("c4_q", [128, SF], BF16)
            k_sb = sb("c4_k", [128, SF], BF16)
            Eb2 = [sb(f"c4_E{i}", [128, SF], BF16) for i in range(2)]
            Zp = sb("c4_Zp", [128, 8], F32)
            z1 = sb("c4_z1", [128, 1], F32)
            iz16 = sb("c4_iz", [128, 1], BF16)
            srow = [sb(f"c4_sr{i}", [1, SF], F32) for i in range(2)]
            SC = [ps(f"SC{i}", [64, 512], F32) for i in range(4)]
            if DEBUG_C:
                dbgz_sb = sb("c4_dbgz", [128, 256], F32)

            srow_st = [0, 0]
            for hd in range(8):
                ldq = sch.dma("sync", q_sb[:], qT_hbm[hd * 128:(hd + 1) * 128, :])
                ldk = sch.dma("sync", k_sb[:], kT_hbm[hd * 128:(hd + 1) * 128, :])
                sch.wait("tensor", "SL", ldk)
                exp_c = [0, 0]
                cs_mm = [0, 0]
                ve_red = 0
                for qt in range(32):
                    ebuf = Eb2[qt % 2]
                    for kc in range(8):
                        pb = [P0, P1][kc % 2]
                        lhsT = q_sb[:, qt * 128:(qt + 1) * 128]
                        rhs = k_sb[:, kc * 512:(kc + 1) * 512]
                        sch.wait("tensor", "SE", exp_c[kc % 2])
                        sch.op("tensor", lambda e, s, l=lhsT, r=rhs, pb=pb: e.matmul(
                            pb[:], l, r, start=True, stop=True), sem="TE")
                        cmm = sch.cnt["TE"]
                        edst = ebuf[:, kc * 512:(kc + 1) * 512]
                        sch.wait("scalar", "TE", cmm)
                        sch.wait("scalar", "TE", cs_mm[qt % 2])
                        sch.wait("scalar", "VE", ve_red)
                        exp_c[kc % 2] = sch.op(
                            "scalar", lambda e, s, ed=edst, pb=pb, kc=kc: e.activation(
                                ed, pb[:], AF.Exp, scale=ISQ128,
                                accum_out=Zp[:, kc:kc + 1]), sem="SE")
                    sch.wait("vector", "SE", exp_c[1])
                    sch.wait("vector", "TE", cs_mm[1 - (qt % 2)])
                    sch.vop(lambda e, s: e.reduce_sum(z1[:], Zp[:], axis=AX.X))
                    sch.vop(lambda e, s: e.reciprocal(z1[:], z1[:]))
                    cvz = sch.vop(lambda e, s: e.tensor_copy(iz16[:], z1[:]))
                    if DEBUG_C:
                        dz = dbgz_sb
                        cvz = sch.op("vector", lambda e, s, hd=hd, qt=qt: e.tensor_copy(
                            dz[:, hd * 32 + qt: hd * 32 + qt + 1], z1[:]), sem="VE")
                    ve_red = cvz
                    sch.wait("tensor", "VE", cvz)
                    for c in range(8):
                        bank, sl = c // 2, c % 2
                        rhs = ebuf[:, c * 512:(c + 1) * 512]
                        st = (qt == 0)
                        sp = (qt == 31)
                        out_ap = SC[bank][sl * 32: sl * 32 + 1, :]
                        sch.op("tensor", lambda e, s, o=out_ap, r=rhs, st=st, sp=sp:
                               e.matmul(o, iz16[:], r, start=st, stop=sp,
                                        skip_group_check=True), sem="TE")
                    cs_mm[qt % 2] = sch.cnt["TE"]
                te_f = sch.cnt["TE"]
                sch.wait("vector", "TE", te_f)
                sch.wait("vector", "SL", srow_st[hd % 2])
                cve = 0
                for c in range(8):
                    bank, sl = c // 2, c % 2
                    dst = srow[hd % 2][0:1, c * 512:(c + 1) * 512]
                    cve = sch.op("vector", lambda e, s, dst=dst, b2=bank, sl=sl:
                                 e.tensor_copy(dst, SC[b2][sl * 32: sl * 32 + 1, :]),
                                 sem="VE")
                sch.wait("sync", "VE", cve)
                srow_st[hd % 2] = sch.dma("sync", scol_hbm[hd:hd + 1, :],
                                          srow[hd % 2][:])
                sch.wait("tensor", "VE", cve)
                sch.wait("sync", "TE", te_f)
            if DEBUG_C:
                sch.wait("sync", "VE", sch.cnt["VE"])
                sch.dma("sync", dbg["zz"], dbgz_sb[:])
            sch.barrier()

        with ExitStack() as pctx:
            sb = lambda *a: pctx.enter_context(nc.sbuf_tensor(*a))
            v_sb = [sb(f"c5_v{i}", [128, SF], BF16) for i in range(2)]
            bsb = sb("c5_b", [128, 512], F32)
            vst = [sb(f"c5_o{i}", [128, SF], BF16) for i in range(2)]
            srow5 = [sb(f"c5_s{i}", [1, SF], F32) for i in range(2)]
            ld_v = [0, 0]
            st_v = [0, 0]
            te_v = [0, 0]
            cvv = 0
            for hd in range(8):
                b = hd % 2
                sch.wait("sync", "TE", te_v[b])
                sch.dma("sync", srow5[b][:], scol_hbm[hd:hd + 1, :])
                ld_v[b] = sch.dma("sync", v_sb[b][:], vT_hbm[hd * 128:(hd + 1) * 128, :])
                sch.wait("tensor", "SL", ld_v[b])
                sch.wait("vector", "SL", ld_v[b])
                sch.wait("vector", "SL", st_v[b])
                for kc in range(8):
                    src = srow5[b][0:1, kc * 512:(kc + 1) * 512]
                    sch.wait("tensor", "VE", cvv)
                    ctb = sch.op("tensor", lambda e, s, src=src: e.matmul(
                        P2[:], ones_row[:], src, start=True, stop=True), sem="TE")
                    sch.wait("vector", "TE", ctb)
                    sch.vop(lambda e, s: e.tensor_copy(bsb[:], P2[:]))
                    dst = vst[b][:, kc * 512:(kc + 1) * 512]
                    vsrc = v_sb[b][:, kc * 512:(kc + 1) * 512]
                    cvv = sch.op("vector", lambda e, s, dst=dst, vs=vsrc: e.tensor_tensor(
                        dst, vs, bsb[:], ALU.mult), sem="VE")
                te_v[b] = sch.cnt["TE"]
                sch.wait("sync", "VE", cvv)
                st_v[b] = sch.dma("sync", valp[hd * 128:(hd + 1) * 128, :], vst[b][:])
            sch.barrier()

        with ExitStack() as pctx:
            sb = lambda *a: pctx.enter_context(nc.sbuf_tensor(*a))
            x1n = sb("m_xn", [128, 16 * 1024], BF16)
            combT1 = sb("m_cT", [1, TO], F32)
            gbuf = sb("m_g", [128, 22 * 1024], BF16)
            wga = [sb(f"m_wg{i}", [128, 16 * 128], BF16) for i in range(2)]
            wua = [sb(f"m_wu{i}", [128, 16 * 128], BF16) for i in range(2)]
            wda = [sb(f"m_wd{i}", [128, 22 * 128], BF16) for i in range(2)]
            routed = sb("m_r", [128, 16 * 1024], F32)
            combB = sb("m_cb", [128, 1024], F32)
            t1 = sb("m_t1", [128, 512], F32)

            x1n_v = x1n_hbm.rearrange("(c p) t -> p c t", p=128)
            moeT_v = moeT.rearrange("(c p) t -> p c t", p=128)
            for p_ in range(2):
                sch.wait("sync", "TE", sch.cnt["TE"])

                def _ldx(e, s, p_=p_):
                    return e.dma_start(
                        out=x1n.ap().rearrange("p (c t) -> p c t", c=16),
                        in_=x1n_v[:, :, bass.ds(reg["coff"] + p_ * 1024, 1024)])
                ldx = sch.op("sync", _ldx, sem="SL", n=16)
                sch.wait("tensor", "SL", ldx)

                te_wg = [0, 0]
                te_wd = [0, 0]
                cse_t1 = 0
                cvm_p1 = 0
                ve_rout = 0

                def gate_up(src_g, src_u, nht, scale_comb):
                    nonlocal cse_t1, cvm_p1
                    for ht in range(nht):
                        wb = ht % 2
                        sch.wait("gpsimd", "TE", te_wg[wb])
                        sch.dma("gpsimd", wga[wb][:], src_g(ht))
                        wld = sch.dma("gpsimd", wua[wb][:], src_u(ht))
                        sch.wait("tensor", "GL", wld)
                        for tc in range(2):
                            sch.wait("tensor", "SE", cse_t1)
                            for dc in range(16):
                                lhsT = wga[wb][:, dc * 128:(dc + 1) * 128]
                                rhs = x1n[:, dc * 1024 + tc * 512: dc * 1024 + (tc + 1) * 512]
                                sch.op("tensor", lambda e, s, l=lhsT, r=rhs, dc=dc:
                                       e.matmul(P0[:], l, r, start=(dc == 0), stop=(dc == 15)),
                                       sem="TE")
                            te_g = sch.cnt["TE"]
                            sch.wait("tensor", "VE", cvm_p1)
                            for dc in range(16):
                                lhsT = wua[wb][:, dc * 128:(dc + 1) * 128]
                                rhs = x1n[:, dc * 1024 + tc * 512: dc * 1024 + (tc + 1) * 512]
                                sch.op("tensor", lambda e, s, l=lhsT, r=rhs, dc=dc:
                                       e.matmul(P1[:], l, r, start=(dc == 0), stop=(dc == 15)),
                                       sem="TE")
                            te_u = sch.cnt["TE"]
                            sch.wait("scalar", "TE", te_g)
                            cse_t1 = sch.op("scalar", lambda e, s: e.activation(
                                t1[:], P0[:], AF.Silu), sem="SE")
                            gdst = gbuf[:, ht * 1024 + tc * 512: ht * 1024 + (tc + 1) * 512]
                            sch.wait("vector", "SE", cse_t1)
                            sch.wait("vector", "TE", te_u)
                            if scale_comb:
                                sch.vop(lambda e, s: e.tensor_tensor(
                                    t1[:], t1[:], P1[:], ALU.mult))
                                cbs = combB[:, tc * 512:(tc + 1) * 512]
                                cvm_p1 = sch.vop(lambda e, s, g=gdst, cb=cbs:
                                                 e.tensor_tensor(g, t1[:], cb, ALU.mult))
                            else:
                                cvm_p1 = sch.vop(lambda e, s, g=gdst:
                                                 e.tensor_tensor(g, t1[:], P1[:], ALU.mult))
                        te_wg[wb] = sch.cnt["TE"]
                    return sch.cnt["VE"]

                def down(src_d, nht, first):
                    nonlocal ve_rout
                    ve_g = sch.cnt["VE"]
                    for fo in range(16):
                        wb = fo % 2
                        sch.wait("gpsimd", "TE", te_wd[wb])
                        wld = sch.dma("gpsimd", wda[wb][:, 0:nht * 128], src_d(fo))
                        sch.wait("tensor", "GL", wld)
                        sch.wait("tensor", "VE", ve_g)
                        for tc in range(2):
                            sch.wait("tensor", "VE", ve_rout)
                            for ht in range(nht):
                                lhsT = wda[wb][:, ht * 128:(ht + 1) * 128]
                                rhs = gbuf[:, ht * 1024 + tc * 512: ht * 1024 + (tc + 1) * 512]
                                sch.op("tensor", lambda e, s, l=lhsT, r=rhs, ht=ht, n=nht:
                                       e.matmul(P2[:], l, r, start=(ht == 0), stop=(ht == n - 1)),
                                       sem="TE")
                            te_d = sch.cnt["TE"]
                            rdst = routed[:, fo * 1024 + tc * 512: fo * 1024 + (tc + 1) * 512]
                            sch.wait("vector", "TE", te_d)
                            if first:
                                ve_rout = sch.op("vector", lambda e, s, r=rdst:
                                                 e.tensor_copy(r, P2[:]), sem="VE")
                            else:
                                ve_rout = sch.op("vector", lambda e, s, r=rdst:
                                                 e.tensor_tensor(r, r, P2[:], ALU.add),
                                                 sem="VE")
                        te_wd[wb] = sch.cnt["TE"]

                gate_up(lambda ht: swg[ht], lambda ht: swu[ht], 22, False)
                down(lambda fo: swd[fo], 22, True)
                for ei in range(8):
                    cvcb = sch.cnt["VE"]
                    sch.wait("sync", "TE", sch.cnt["TE"])
                    ld_ct = sch.dma("sync", combT1[:],
                                    combT_hbm[0:1, ei * TO:(ei + 1) * TO])
                    sch.wait("tensor", "SL", ld_ct)
                    for tc in range(2):
                        src = combT1[0:1, p_ * 1024 + tc * 512:
                                     p_ * 1024 + (tc + 1) * 512]
                        cvcb = sch.cnt["VE"]
                        sch.wait("tensor", "VE", cvcb)
                        ctb = sch.op("tensor", lambda e, s, src=src: e.matmul(
                            P3[:], ones_row[:], src, start=True, stop=True), sem="TE")
                        sch.wait("vector", "TE", ctb)
                        dst = combB[:, tc * 512:(tc + 1) * 512]
                        sch.vop(lambda e, s, dst=dst: e.tensor_copy(
                            dst, P3[:]))
                    gate_up(lambda ht, ei=ei: wgm[ei, ht],
                            lambda ht, ei=ei: wum[ei, ht], 11, True)
                    down(lambda fo, ei=ei: wdm[ei, fo], 11, False)
                cfin = sch.cnt["VE"]
                sch.wait("sync", "VE", cfin)
                sch.dma("sync", moeT_v[:, :, p_ * 1024:(p_ + 1) * 1024],
                        routed.ap().rearrange("p (c t) -> p c t", c=16))
            sch.barrier()

        if DEBUG_C:
            sch.dma("sync", dbg["xn"], x1n_hbm)
            sch.dma("sync", dbg["q"], qT_hbm)
            sch.dma("sync", dbg["k"], kT_hbm)
            sch.dma("sync", dbg["v"], vT_hbm)
            sch.dma("sync", dbg["scol"], scol_hbm)
            sch.dma("sync", dbg["comb"], combT_hbm)
            sch.dma("sync", dbg["stats"], stats_hbm)
        sch.finish()
        sch.run(nc, sems)
    return nc



def build_D():
    nc = bass.Bass("TRN2", target_bir_lowering=False)
    valb = nc.dram_tensor("valb", [1, 128, 16 * TO], BF16, kind="ExternalInput").ap()
    wo1 = nc.dram_tensor("wo1", [16, 128, 16 * 128], F32, kind="ExternalInput").ap()
    x1o = nc.dram_tensor("x1o", [D, TO], F32, kind="ExternalInput").ap()
    moe = nc.dram_tensor("moe", [D, TO], F32, kind="ExternalInput").ap()
    outT = nc.dram_tensor("outT", [D, TO], F32, kind="ExternalOutput").ap()

    sch = Sched()
    with ExitStack() as octx:
        sems = _sems(nc, octx)
        sb = lambda *a: octx.enter_context(nc.sbuf_tensor(*a))
        ps = lambda *a: octx.enter_context(nc.psum_tensor(*a))
        val_sb = sb("d_val", [128, 16 * TO], BF16)
        wo_sb = [sb(f"d_wo{i}", [128, 16 * 128], BF16) for i in range(2)]
        xfo = [sb(f"d_x{i}", [128, TO], F32) for i in range(2)]
        mfo = [sb(f"d_m{i}", [128, TO], F32) for i in range(2)]
        ost = [sb(f"d_o{i}", [128, TO], F32) for i in range(2)]
        P0 = ps("P0", [128, 512], F32)
        P1 = ps("P1", [128, 512], F32)

        ld_v = sch.dma("sync", val_sb[:], valb[0])
        sch.wait("tensor", "SL", ld_v)

        x_v = x1o.rearrange("(c p) t -> p c t", p=128)
        m_v = moe.rearrange("(c p) t -> p c t", p=128)
        o_v = outT.rearrange("(c p) t -> p c t", p=128)

        te_w = [0, 0]
        ld_xf = [0, 0]
        out_st = [0, 0]
        ve_p = [0, 0]
        for fo in range(16):
            wb = fo % 2
            sch.wait("gpsimd", "TE", te_w[wb])
            wld = sch.dma("gpsimd", wo_sb[wb][:], wo1[fo])
            sch.wait("sync", "VE", ve_p[1])
            sch.dma("sync", xfo[wb][:], x_v[:, fo, :])
            ld_xf[wb] = sch.dma("sync", mfo[wb][:], m_v[:, fo, :])
            sch.wait("tensor", "GL", wld)
            for qc in range(4):
                pb = [P0, P1][qc % 2]
                sch.wait("tensor", "VE", ve_p[qc % 2])
                for cc in range(16):
                    lhsT = wo_sb[wb][:, cc * 128:(cc + 1) * 128]
                    rhs = val_sb[:, cc * TO + qc * 512: cc * TO + (qc + 1) * 512]
                    sch.op("tensor", lambda e, s, l=lhsT, r=rhs, cc=cc, pb=pb: e.matmul(
                        pb[:], l, r, start=(cc == 0), stop=(cc == 15)), sem="TE")
                te_w[wb] = sch.cnt["TE"]
                sch.wait("vector", "TE", te_w[wb])
                sch.wait("vector", "SL", ld_xf[wb])
                sch.wait("vector", "SL", out_st[wb])
                dst = ost[wb][:, qc * 512:(qc + 1) * 512]
                xs = xfo[wb][:, qc * 512:(qc + 1) * 512]
                ms = mfo[wb][:, qc * 512:(qc + 1) * 512]
                sch.vop(lambda e, s, dst=dst, xs=xs, pb=pb: e.tensor_tensor(
                    dst, pb[:], xs, ALU.add))
                ve_p[qc % 2] = sch.vop(
                    lambda e, s, dst=dst, ms=ms: e.tensor_tensor(
                        dst, dst, ms, ALU.add))
            sch.wait("sync", "VE", ve_p[1])
            out_st[wb] = sch.dma("sync", o_v[:, fo, :], ost[wb][:])
        sch.finish()
        sch.run(nc, sems)
    return nc



def _blk(w, sw):
    din, dout = w.shape
    return np.ascontiguousarray(
        w.reshape(din // 128, 128, dout // sw, sw).transpose(2, 1, 0, 3)
    ).reshape(dout // sw, 128, (din // 128) * sw)


def _masks_host():
    m = np.zeros((128, 4, 512), dtype=np.float32)
    tri = (np.arange(128)[None, :] >= np.arange(128)[:, None]).astype(np.float32)
    for p in range(4):
        for s_ in range(4):
            if p < s_:
                m[:, p, s_ * 128:(s_ + 1) * 128] = 1.0
            elif p == s_:
                m[:, p, s_ * 128:(s_ + 1) * 128] = tri
    return m.reshape(128, 4 * 512).astype(ml_dtypes.bfloat16)


_CACHE = {}


def _programs():
    if "A" not in _CACHE:
        _CACHE["A"] = build_A()
        _CACHE["B"] = build_B()
        _CACHE["C"] = build_C()
        _CACHE["D"] = build_D()
    return _CACHE


def kernel(L0_attn_x, L0_attn_Wq, L0_attn_Wk, L0_attn_Wv, L0_attn_Wo,
           L0_ffn_w_gate, L0_ffn_w_up, L0_ffn_w_down,
           L1_attn_w_qkv, L1_attn_w_o,
           L1_ffn_gate_w, L1_ffn_w_gate, L1_ffn_w_up, L1_ffn_w_down,
           L1_ffn_sw_gate, L1_ffn_sw_up, L1_ffn_sw_down):
    f32 = np.float32
    x = np.asarray(L0_attn_x, f32)
    WqT = np.asarray(L0_attn_Wq, f32).T
    WkT = np.asarray(L0_attn_Wk, f32).T
    WvT = np.asarray(L0_attn_Wv, f32).T
    WoT = np.asarray(L0_attn_Wo, f32).T
    wg0 = np.asarray(L0_ffn_w_gate, f32)
    wu0 = np.asarray(L0_ffn_w_up, f32)
    wd0 = np.asarray(L0_ffn_w_down, f32)
    wqkv = np.asarray(L1_attn_w_qkv, f32)
    wo1 = np.asarray(L1_attn_w_o, f32)
    gw = np.asarray(L1_ffn_gate_w, f32)
    wgm = np.asarray(L1_ffn_w_gate, f32)
    wum = np.asarray(L1_ffn_w_up, f32)
    wdm = np.asarray(L1_ffn_w_down, f32)
    swg = np.asarray(L1_ffn_sw_gate, f32)
    swu = np.asarray(L1_ffn_sw_up, f32)
    swd = np.asarray(L1_ffn_sw_down, f32)

    progs = _programs()
    masks = _masks_host()
    idn16 = np.eye(128, dtype=ml_dtypes.bfloat16)
    idn32 = np.eye(128, dtype=f32)

    wqA = [_blk(WqT[:, h * 1024:(h + 1) * 1024], 1024) for h in range(2)]
    wkA = [_blk(WkT[:, h * 256:(h + 1) * 256], 256) for h in range(2)]
    wvA = [_blk(WvT[:, h * 256:(h + 1) * 256], 256) for h in range(2)]
    wgateA = _blk(wg0, 512)
    wupA = _blk(wu0, 512)
    wdownA = _blk(wd0, 128)
    woB = _blk(WoT, 128)
    wq1C = [_blk(wqkv[:, h * 1024:(h + 1) * 1024], 1024) for h in range(2)]
    wk1C = [_blk(wqkv[:, 2048 + h * 1024: 2048 + (h + 1) * 1024], 1024) for h in range(2)]
    wv1C = [_blk(wqkv[:, 4096 + h * 1024: 4096 + (h + 1) * 1024], 1024) for h in range(2)]
    gwC = _blk(gw, 8)
    wgmC = np.stack([_blk(wgm[e], 128).reshape(11, 128, 16 * 128) for e in range(8)])
    wumC = np.stack([_blk(wum[e], 128).reshape(11, 128, 16 * 128) for e in range(8)])
    wdmC = np.stack([_blk(wdm[e], 128) for e in range(8)])
    swgC = _blk(swg, 128)
    swuC = _blk(swu, 128)
    swdC = _blk(swd, 128)
    wo1D = _blk(wo1, 128)

    coffs = [np.array([[(c % 2) * TO]], dtype=np.int32) for c in range(NCORES)]
    xTs = [np.ascontiguousarray(x[c // 2].T) for c in range(NCORES)]

    inA = [dict(xT=xTs[c], wq=wqA[c % 2], wk=wkA[c % 2],
                wv=wvA[c % 2], wgate=wgateA, wup=wupA, wdown=wdownA,
                masks=masks, idn16=idn16, coff=coffs[c])
           for c in range(NCORES)]
    resA = run_bass_kernel_spmd(progs["A"], inA, core_ids=list(range(NCORES)))
    ctxpA = [np.asarray(resA.results[c]["ctxp"]) for c in range(NCORES)]
    ffn0A = [np.asarray(resA.results[c]["ffn0"]) for c in range(NCORES)]

    def own_cols(c):
        h = c % 2
        return slice(h * TO, (h + 1) * TO)

    inB = []
    for c in range(NCORES):
        b, h = c // 2, c % 2
        ctx_both = np.concatenate([ctxpA[2 * b][:, own_cols(c)],
                                   ctxpA[2 * b + 1][:, own_cols(c)]], axis=0)
        ctx_blk = np.ascontiguousarray(
            ctx_both.reshape(16, 128, TO).transpose(1, 0, 2)).reshape(1, 128, 16 * TO)
        xo = np.ascontiguousarray(xTs[c][:, own_cols(c)])
        inB.append(dict(ctxb=ctx_blk, wo=woB, xo=xo, ffn=ffn0A[c]))
    resB = run_bass_kernel_spmd(progs["B"], inB, core_ids=list(range(NCORES)))
    x1B = [np.asarray(resB.results[c]["x1"]) for c in range(NCORES)]

    inC = []
    for c in range(NCORES):
        b, h = c // 2, c % 2
        x1full = np.ascontiguousarray(np.concatenate([x1B[2 * b], x1B[2 * b + 1]], axis=1))
        x1o_blk = np.ascontiguousarray(
            x1B[c].reshape(16, 128, 16, 128).transpose(2, 1, 0, 3)
        ).reshape(16, 128, 16 * 128)
        inC.append(dict(x1T=x1full, x1o=x1o_blk,
                        wq1=wq1C[h], wk1=wk1C[h], wv1=wv1C[h],
                        gw=gwC,
                        wgm=wgmC, wum=wumC, wdm=wdmC,
                        swg=swgC, swu=swuC, swd=swdC,
                        idn32=idn32, coff=coffs[c]))
    resC = run_bass_kernel_spmd(progs["C"], inC, core_ids=list(range(NCORES)))
    valpC = [np.asarray(resC.results[c]["valp"]) for c in range(NCORES)]
    moeC = [np.asarray(resC.results[c]["moeT"]) for c in range(NCORES)]

    inD = []
    for c in range(NCORES):
        b, h = c // 2, c % 2
        val_both = np.concatenate([valpC[2 * b][:, own_cols(c)],
                                   valpC[2 * b + 1][:, own_cols(c)]], axis=0)
        val_blk = np.ascontiguousarray(
            val_both.reshape(16, 128, TO).transpose(1, 0, 2)).reshape(1, 128, 16 * TO)
        x1o = np.ascontiguousarray(x1B[c])
        inD.append(dict(valb=val_blk, wo1=wo1D, x1o=x1o, moe=moeC[c]))
    resD = run_bass_kernel_spmd(progs["D"], inD, core_ids=list(range(NCORES)))

    out = np.empty((4, SF, D), dtype=f32)
    for c in range(NCORES):
        b, h = c // 2, c % 2
        out[b, h * TO:(h + 1) * TO, :] = np.asarray(resD.results[c]["outT"]).T
    return out
